# version 20
# baseline (speedup 1.0000x reference)
import numpy as np
from contextlib import ExitStack

import concourse.bass as bass
import concourse.mybir as mybir
from concourse.bass_utils import run_bass_kernel_spmd

F32 = mybir.dt.float32
F32R = mybir.dt.float32r
BF16 = mybir.dt.bfloat16
AF = mybir.ActivationFunctionType
OP = mybir.AluOpType

NCORE = 8
L = 4096
DM = 1024
N = 16
RDT = 64
NDB = 8
TC = 1024
NCH = L // TC
SUB = 512
DBG = 2
NG = NDB // DBG
EPS = 1e-5


class Plan:

    ENGINES = ("sync", "gpsimd", "scalar", "vector", "tensor")

    def __init__(self):
        self.streams = {e: [] for e in self.ENGINES}
        self.cnt = {}

    def new_sem(self, name):
        self.cnt[name] = 0
        return name

    def op(self, eng, emit, waits=(), inc=None, inc_amt=1):
        tok = None
        if inc is not None:
            self.cnt[inc] += inc_amt
            tok = (inc, self.cnt[inc])
        self.streams[eng].append((tuple(waits), emit, inc, inc_amt))
        return tok

    def dma(self, eng, out, in_, sem, waits=()):
        def emit(e, out=out, in_=in_):
            return e.dma_start(out=out, in_=in_)
        return self.op(eng, emit, waits=waits, inc=sem, inc_amt=16)


def build_program(debug=False):
    nc = bass.Bass("TRN2", target_bir_lowering=False,
                   detect_race_conditions=False)

    d_x = nc.dram_tensor("x_t", [DM, L], F32, kind="ExternalInput").ap()
    d_win = nc.dram_tensor("w_in", [DM, 2 * N], F32, kind="ExternalInput").ap()
    d_bin = nc.dram_tensor("b_in_c", [2 * N, 1], F32, kind="ExternalInput").ap()
    d_wdt1 = nc.dram_tensor("w_dt1", [DM, RDT], F32, kind="ExternalInput").ap()
    d_bdt1 = nc.dram_tensor("b_dt1_c", [RDT, 1], F32, kind="ExternalInput").ap()
    d_wdt2 = nc.dram_tensor("w_dt2", [RDT, DM], F32, kind="ExternalInput").ap()
    d_bdt2 = nc.dram_tensor("b_dt2_c", [128, NDB], F32, kind="ExternalInput").ap()
    d_amat = nc.dram_tensor("a_mat", [DM, N], F32, kind="ExternalInput").ap()
    d_dvec = nc.dram_tensor("d_vec", [128, NDB], F32, kind="ExternalInput").ap()
    d_wffn = nc.dram_tensor("wt_ffn", [DM, DM], F32, kind="ExternalInput").ap()
    d_bffn = nc.dram_tensor("b_ffn_c", [128, NDB], F32, kind="ExternalInput").ap()
    d_ldown = nc.dram_tensor("lora_down_m", [4, 100], F32, kind="ExternalInput").ap()
    d_lup = nc.dram_tensor("lora_up_m", [50, 4], F32, kind="ExternalInput").ap()
    d_uaxt = nc.dram_tensor("up_aux_t", [50, DM], F32, kind="ExternalInput").ap()
    d_daux = nc.dram_tensor("down_aux_m", [100, DM], F32, kind="ExternalInput").ap()
    d_ident = nc.dram_tensor("ident01", [128, 128], F32, kind="ExternalInput").ap()
    d_onesr = nc.dram_tensor("ones_row", [1, 128], F32, kind="ExternalInput").ap()
    d_onesc = nc.dram_tensor("ones_col", [128, 8], F32, kind="ExternalInput").ap()
    d_eps = nc.dram_tensor("eps_col", [128, 1], F32, kind="ExternalInput").ap()
    d_out = nc.dram_tensor("out_t", [DM, L], F32, kind="ExternalOutput").ap()
    d_weff = nc.dram_tensor("weff_sc", [128, NDB, NDB, 128], BF16, kind="Internal").ap()

    x_v = d_x.rearrange("(db p) l -> p db l", p=128)
    out_v = d_out.rearrange("(db p) l -> p db l", p=128)
    amat_v = d_amat.rearrange("(db p) n -> p db n", p=128)
    win_v = d_win.rearrange("(db p) m -> p db m", p=128)
    wdt1_v = d_wdt1.rearrange("(db p) m -> p db m", p=128)
    wdt2_v = d_wdt2.rearrange("r (db p) -> r db p", p=128)
    wffn_v = d_wffn.rearrange("(kb p) o -> p kb o", p=128)

    es = ExitStack()
    sb = lambda name, shape, dt: es.enter_context(nc.sbuf_tensor(name, shape, dt))
    psb = lambda name, shape: es.enter_context(nc.psum_tensor(name, shape, F32))

    s_xbuf = sb("xbuf", [128, NDB, TC], F32)
    s_xr = sb("xr", [128, NDB, TC], F32R)
    s_lnx = sb("lnx", [128, NDB, TC], F32R)
    s_dt = sb("dtout", [128, NDB, TC], F32)
    s_a = [sb(f"a{i}", [128, DBG, TC], F32) for i in range(2)]
    s_u = sb("u", [128, DBG, TC], F32)
    s_h = sb("hh", [128, DBG, TC], F32)
    s_w = sb("ww", [128, DBG, TC], F32R)
    s_tsq = sb("tsq", [128, 4, SUB], F32R)
    s_bcc = sb("bcc", [2 * N, TC], F32R)
    s_z1 = sb("z1c", [RDT, TC], F32R)
    s_ccc = sb("ccc", [N, TC], F32R)
    s_winr = sb("winr", [128, NDB, 2 * N], F32R)
    s_wdt1r = sb("wdt1r", [128, NDB, RDT], F32R)
    s_wdt2r = sb("wdt2r", [RDT, NDB, 128], F32R)
    s_small = sb("smallv", [128, 4 + 3 * NDB], F32)
    s_binc = s_small[0:2 * N, 0:1]
    s_bdt1c = s_small[0:RDT, 1:2]
    s_wsin = s_small[0:2 * N, 2:3]
    s_wsdt1 = s_small[0:RDT, 3:4]
    s_bdt2c = s_small[:, 4:4 + NDB]
    s_dvec = s_small[:, 4 + NDB:4 + 2 * NDB]
    s_bffn = s_small[:, 4 + 2 * NDB:4 + 3 * NDB]
    s_amat = sb("amat", [128, NDB, N], F32)
    s_inva = sb("inva", [128, NDB, N], F32)
    s_identr = sb("identr", [128, 128], F32R)
    s_ident01 = s_identr
    s_onesr = sb("onesrr", [1, 128], F32R)
    s_onesc = sb("onescr", [128, 8], F32R)
    s_carry = sb("carry", [128, N, NDB], F32)
    s_epsc = sb("epsc", [128, 1], F32)
    s_ddg = sb("ddg", [128, DBG, 128], F32R)
    s_wslab = sb("wslab", [128, 2, NDB, 128], BF16)

    s_mbrb = s_w[:]
    s_l1 = s_lnx[0:4, 0, 0:DM]
    s_l2 = s_lnx[0:100, 1, 0:DM]
    s_wtmp = [s_xbuf[:, 6, :], s_xbuf[:, 7, :]]
    s_lup = s_lnx[0:50, 2, 0:4]
    s_ldn = s_lnx[0:4, 3, 0:100]
    s_uax = s_w[0:50, 0, :]
    s_dax = s_lnx[0:100, 6, 0:DM]

    row_m = s_u[0:1, 0, :]
    row_e2 = s_u[0:1, 1, :]
    s_sp1 = s_h[:, 1, :]
    row_msq = s_sp1[0:1, 0:TC]
    row_var = row_e2
    s_sp2 = s_u[:, 0, :]
    s_rstd = s_tsq[0:1, 0:2, :].rearrange("p a b -> p (a b)")
    s_mrn = s_ccc[0:1, :]

    psA = psb("psA", [128, TC])
    psB = psb("psB", [128, TC])
    psC = psb("psC", [128, TC])
    psD = psb("psD", [128, TC])

    pl = Plan()
    SDW = pl.new_sem("sdw")
    SGW = pl.new_sem("sgw")
    SWT = pl.new_sem("swt")
    SDF = pl.new_sem("sdf")
    SDX = pl.new_sem("sdx")
    SGX = pl.new_sem("sgx")
    SDC = pl.new_sem("sdc")
    SXC = pl.new_sem("sxc")
    SDS = [pl.new_sem("sds0"), pl.new_sem("sds1")]
    ST = pl.new_sem("st_")
    SS = pl.new_sem("ss")
    SV = pl.new_sem("sv")
    SO = pl.new_sem("so")

    W = lambda *toks: tuple(t for t in toks if t is not None)

    t_ld = []
    t_ld.append(pl.dma("sync", s_binc[:], d_bin, SDW))
    t_ld.append(pl.dma("sync", s_bdt1c[:], d_bdt1, SDW))
    t_ld.append(pl.dma("sync", s_bdt2c[:], d_bdt2, SDW))
    t_ld.append(pl.dma("sync", s_amat[:], amat_v, SDW))
    t_ld.append(pl.dma("sync", s_dvec[:], d_dvec, SDW))
    t_ld.append(pl.dma("sync", s_bffn[:], d_bffn, SDW))
    t_ld.append(pl.dma("sync", s_epsc[:], d_eps, SDW))
    t_lg = []
    t_lg.append(pl.dma("gpsimd", s_winr[:], win_v, SGW))
    t_lg.append(pl.dma("gpsimd", s_wdt1r[:], wdt1_v, SGW))
    t_lg.append(pl.dma("gpsimd", s_wdt2r[:], wdt2_v, SGW))
    t_lg.append(pl.dma("gpsimd", s_identr[:], d_ident, SGW))
    t_lg.append(pl.dma("gpsimd", s_onesr[:], d_onesr, SGW))
    t_lg.append(pl.dma("gpsimd", s_onesc[:], d_onesc, SGW))
    t_lg.append(pl.dma("gpsimd", s_lup[:], d_lup, SGW))
    t_lg.append(pl.dma("gpsimd", s_ldn[:], d_ldown, SGW))
    t_lg.append(pl.dma("gpsimd", s_uax[:], d_uaxt, SGW))
    t_lg.append(pl.dma("gpsimd", s_dax[:], d_daux, SGW))
    tok_ld = t_ld[-1]
    tok_lg = tok_lg2 = t_lg[-1]
    tok_xr0 = pl.dma("gpsimd", s_xr[:], x_v[:, :, 0:TC], SGX)

    tok_carry0 = pl.op("vector", lambda v: v.memset(s_carry[:], 0.0), inc=SV)
    tok_inva = pl.op(
        "vector",
        lambda v: v.reciprocal(
            out=s_inva[:].rearrange("p a b -> p (a b)"),
            in_=s_amat[:].rearrange("p a b -> p (a b)")),
        waits=W(tok_ld), inc=SV)
    for db in range(NDB):
        pl.op("tensor",
              lambda t, db=db: t.matmul(psA[0:2 * N, 0:8], s_winr[:, db], s_onesc[:],
                                        start=(db == 0), stop=(db == NDB - 1)),
              waits=W(tok_lg) if db == 0 else ())
    tok_t = pl.op("tensor",
                  lambda t: t.matmul(psB[0:RDT, 0:8], s_wdt1r[:, 0], s_onesc[:],
                                     start=True, stop=False),
                  inc=ST)
    for db in range(1, NDB):
        tok_t = pl.op("tensor",
                      lambda t, db=db: t.matmul(psB[0:RDT, 0:8], s_wdt1r[:, db],
                                                s_onesc[:], start=False,
                                                stop=(db == NDB - 1)),
                      inc=ST)
    tok_ws = pl.op("vector", lambda v: v.tensor_copy(out=s_wsin[:], in_=psA[0:2 * N, 0:1]),
                   waits=W(tok_t), inc=SV)
    tok_ws = pl.op("vector", lambda v: v.tensor_copy(out=s_wsdt1[:], in_=psB[0:RDT, 0:1]),
                   inc=SV)

    tok_t = None
    for sub in range(2):
        tok_t = pl.op("tensor",
                      lambda t, sub=sub: t.matmul(
                          psA[0:4, sub * SUB:(sub + 1) * SUB], s_lup[:],
                          s_uax[:, sub * SUB:(sub + 1) * SUB], start=True, stop=True),
                      waits=W(tok_lg2, tok_ws), inc=ST)
    tok_v = pl.op("vector", lambda v: v.tensor_copy(out=s_l1[:], in_=psA[0:4, :]),
                  waits=W(tok_t), inc=SV)
    tok_t = None
    for sub in range(2):
        tok_t = pl.op("tensor",
                      lambda t, sub=sub: t.matmul(
                          psA[0:100, sub * SUB:(sub + 1) * SUB], s_ldn[:],
                          s_l1[:, sub * SUB:(sub + 1) * SUB], start=True, stop=True),
                      waits=W(tok_v), inc=ST)
    tok_v = pl.op("vector", lambda v: v.tensor_copy(out=s_l2[:], in_=psA[0:100, :]),
                  waits=W(tok_t), inc=SV)
    tok_wf_dma = None
    wtmpb = [s_dt[:, 0].bitcast(BF16)[:, 0:DM],
             s_dt[:, 0].bitcast(BF16)[:, DM:2 * DM]]
    tok_vw_k = {}
    tok_wf_k = {}
    for kb in range(NDB):
        par = kb % 2
        tok_g = pl.dma("sync", s_wtmp[par][:], wffn_v[:, kb, :], SWT,
                       waits=W(tok_vw_k.get(kb - 2)))
        tok_t = None
        for sub in range(2):
            tok_t = pl.op("tensor",
                          lambda t, kb=kb, sub=sub: t.matmul(
                              psB[:, sub * SUB:(sub + 1) * SUB],
                              s_dax[:, kb * 128:(kb + 1) * 128],
                              s_l2[:, sub * SUB:(sub + 1) * SUB],
                              start=True, stop=True),
                          waits=W(tok_v) if (kb == 0 and sub == 0) else
                          (W(tok_vw_k.get(kb - 1)) if sub == 0 else ()),
                          inc=ST)
        tok_vw_k[kb] = pl.op("vector",
                             lambda v, kb=kb, par=par: v.tensor_add(
                                 out=wtmpb[par], in0=s_wtmp[par][:],
                                 in1=psB[:, :]),
                             waits=W(tok_t, tok_g, tok_wf_k.get(kb - 2)), inc=SV)
        tok_wf_k[kb] = tok_wf_dma = pl.dma(
            "sync", d_weff[:, :, kb, :],
            wtmpb[par].rearrange("p (ob ow) -> p ob ow", ow=128),
            SDF, waits=W(tok_vw_k[kb]))
    tok_wfold_v = tok_vw_k[NDB - 1]

    rb_b = s_mbrb[:, 0].unsqueeze(1).broadcast_to([128, NDB, TC])
    mrn_b = s_mbrb[:, 1].unsqueeze(1).broadcast_to([128, NDB, TC])
    rb4 = s_mbrb[:, 0].unsqueeze(1).broadcast_to([128, 4, TC])
    mrn4 = s_mbrb[:, 1].unsqueeze(1).broadcast_to([128, 4, TC])
    l2b = [s_a[i][:].bitcast(BF16).rearrange("p a b -> p (a b)")
           .rearrange("p (a b) -> p a b", b=TC) for i in range(2)]

    ck = [dict() for _ in range(NCH + 1)]
    ck[0]["wf_dma"] = tok_wf_dma

    def P0_xr(c):
        if c == 0:
            ck[0]["xr"] = tok_xr0
            return
        p = ck[c - 1]
        ck[c]["xr"] = pl.dma(
            "gpsimd", s_xr[:], x_v[:, :, c * TC:(c + 1) * TC], SGX,
            waits=W(p.get("lnx"), p.get("bcraw"), p.get("stats")))

    def P0_x(c):
        p = ck[c - 1] if c > 0 else {}
        ck[c]["x"] = pl.dma(
            "sync", s_xbuf[:], x_v[:, :, c * TC:(c + 1) * TC], SDX,
            waits=W(p.get("x2_done"), p.get("sqS"), p.get("x2r_read"),
                    p.get("ln2"), tok_wfold_v if c == 0 else None))

    def P1(c):
        p = ck[c - 1] if c > 0 else {}
        tok_xr = ck[c]["xr"]
        tok_sq = {}
        tok_stats = None
        for sub in range(2):
            ssl = slice(sub * SUB, (sub + 1) * SUB)
            for db in range(NDB):
                par = db % 2
                prev = tok_sq.get(sub * NDB + db - 2)
                tok_s = pl.op(
                    "scalar",
                    lambda s, db=db, ssl=ssl, par=par: s.activation(
                        out=s_tsq[:, par], in_=s_xr[:, db, ssl], func=AF.Square),
                    waits=W(tok_xr, prev,
                            p.get("bc15") if (sub == 0 and db == 0) else None),
                    inc=SS)
                tok_sq[sub * NDB + db] = pl.op(
                    "tensor",
                    lambda t, db=db, sub=sub, par=par: t.matmul(
                        psD[0:1, sub * SUB:(sub + 1) * SUB], s_onesc[:, 0:1],
                        s_tsq[:, par], start=(db == 0), stop=(db == NDB - 1)),
                    waits=W(tok_s, p.get("mb5") if db == 0 else None), inc=ST)
                tok_stats = pl.op(
                    "tensor",
                    lambda t, db=db, sub=sub, ssl=ssl: t.matmul(
                        psC[0:1, sub * SUB:(sub + 1) * SUB], s_onesc[:, 0:1],
                        s_xr[:, db, ssl], start=(db == 0), stop=(db == NDB - 1)),
                    waits=W(tok_xr, p.get("mb5")) if db == 0 else (), inc=ST)
        ck[c]["stats"] = tok_stats
        tok_r = pl.op("vector", lambda v: v.tensor_scalar(
            out=row_m, in0=psC[0:1, :], scalar1=1.0 / DM, scalar2=None,
            op0=OP.mult), waits=W(tok_stats, tok_sq[2 * NDB - 1]), inc=SV)
        tok_r = pl.op("vector", lambda v: v.tensor_scalar(
            out=row_e2, in0=psD[0:1, :], scalar1=1.0 / DM, scalar2=None,
            op0=OP.mult), inc=SV)
        tok_msq = pl.op("scalar", lambda s: s.activation(
            out=row_msq, in_=row_m, func=AF.Square),
            waits=W(tok_r), inc=SS)
        tok_var = pl.op("vector", lambda v: v.tensor_sub(
            out=row_var, in0=row_e2, in1=row_msq),
            waits=W(tok_msq), inc=SV)
        pl.op("scalar", lambda s: s.activation(
            out=row_msq, in_=row_var, func=AF.Ln, bias=s_epsc[0:1, :]),
            waits=W(tok_var), inc=SS)
        tok_rstd = pl.op("scalar", lambda s: s.activation(
            out=s_rstd[:], in_=row_msq, func=AF.Exp, scale=-0.5), inc=SS)
        tok_mrn = pl.op("vector", lambda v: v.scalar_tensor_tensor(
            out=s_mrn[:], in0=row_m, scalar=-1.0, in1=s_rstd[:],
            op0=OP.mult, op1=OP.mult), waits=W(tok_rstd), inc=SV)
        tok_bc1 = None
        for sub in range(2):
            ssl = slice(sub * SUB, (sub + 1) * SUB)
            tok_bc1 = pl.op("tensor", lambda t, ssl=ssl: t.matmul(
                psC[:, ssl], s_onesr[:], s_rstd[:, ssl], start=True, stop=True),
                waits=W(tok_rstd, tok_mrn), inc=ST)
            tok_bc1 = pl.op("tensor", lambda t, ssl=ssl: t.matmul(
                psD[:, ssl], s_onesr[:], s_mrn[:, ssl], start=True, stop=True),
                inc=ST)
        ck[c]["bc1"] = tok_bc1
        tok_mb = pl.op("vector", lambda v: v.tensor_copy(out=s_mbrb[:, 0], in_=psC[:, :]),
                       waits=W(tok_bc1), inc=SV)
        tok_mb = pl.op("vector", lambda v: v.tensor_copy(out=s_mbrb[:, 1], in_=psD[:, :]),
                       inc=SV)
        ck[c]["mb"] = tok_mb

    def P2(c):
        tok_mb = ck[c]["mb"]
        tok_bcraw = None
        for sub in range(2):
            ssl = slice(sub * SUB, (sub + 1) * SUB)
            for db in range(NDB):
                tok_bcraw = pl.op("tensor", lambda t, db=db, ssl=ssl: t.matmul(
                    psC[0:2 * N, ssl], s_winr[:, db], s_xr[:, db, ssl],
                    start=(db == 0), stop=(db == NDB - 1)),
                    waits=W(tok_mb if (sub == 0 and db == 0) else None), inc=ST)
                tok_bcraw = pl.op("tensor", lambda t, db=db, ssl=ssl: t.matmul(
                    psD[0:RDT, ssl], s_wdt1r[:, db], s_xr[:, db, ssl],
                    start=(db == 0), stop=(db == NDB - 1)), inc=ST)
        ck[c]["bcraw"] = tok_bcraw
        tok_bcc = pl.op("vector", lambda v: v.tensor_mul(
            out=s_bcc[:], in0=psC[0:2 * N, :], in1=s_mbrb[0:2 * N, 0]),
            waits=W(tok_bcraw, tok_mb), inc=SV)
        tok_bcc = pl.op("vector", lambda v: v.scalar_tensor_tensor(
            out=s_bcc[:], in0=s_mbrb[0:2 * N, 1], scalar=s_wsin[:], in1=s_bcc[:],
            op0=OP.mult, op1=OP.add), inc=SV)
        tok_bcc = pl.op("vector", lambda v: v.tensor_scalar(
            out=s_bcc[:], in0=s_bcc[:], scalar1=s_binc[:], scalar2=None,
            op0=OP.add), inc=SV)
        ck[c]["bcc"] = tok_bcc
        tok_z1 = pl.op("vector", lambda v: v.tensor_mul(
            out=s_z1[:], in0=psD[0:RDT, :], in1=s_mbrb[0:RDT, 0]), inc=SV)
        tok_z1 = pl.op("vector", lambda v: v.scalar_tensor_tensor(
            out=s_z1[:], in0=s_mbrb[0:RDT, 1], scalar=s_wsdt1[:], in1=s_z1[:],
            op0=OP.mult, op1=OP.add), inc=SV)
        tok_z1 = pl.op("vector", lambda v: v.tensor_scalar(
            out=s_z1[:], in0=s_z1[:], scalar1=s_bdt1c[:], scalar2=None,
            op0=OP.add), inc=SV)
        ck[c]["z1"] = tok_z1
        ck[c]["cc"] = pl.dma("sync", s_ccc[:], s_bcc[N:2 * N, :], SDC,
                             waits=W(tok_bcc))

    def LNX(c):
        p = ck[c - 1] if c > 0 else {}
        tok_lnx = pl.op("vector", lambda v: v.tensor_mul(
            out=s_lnx[:], in0=s_xr[:], in1=rb_b),
            waits=W(ck[c]["mb"], p.get("dterm3"), p.get("x2r_read"),
                    ck[c]["xr"]), inc=SV)
        tok_lnx = pl.op("vector", lambda v: v.tensor_add(
            out=s_lnx[:], in0=s_lnx[:], in1=mrn_b), inc=SV)
        ck[c]["lnx"] = tok_lnx

    def P3(c):
        p = ck[c - 1] if c > 0 else {}
        tok_z1 = ck[c]["z1"]
        tok_dt = {}
        for db in range(NDB):
            pz = psC if db % 2 == 0 else psD
            w0 = W(tok_z1, ck[c]["bcc"]) if db < 2 else W(tok_dt.get(db - 2))
            tok_zmm = None
            for sub in range(2):
                ssl = slice(sub * SUB, (sub + 1) * SUB)
                tok_zmm = pl.op("tensor", lambda t, db=db, ssl=ssl, pz=pz: t.matmul(
                    pz[:, ssl], s_wdt2r[:, db, :], s_z1[:, ssl],
                    start=True, stop=True), waits=w0 if sub == 0 else (), inc=ST)
            par = db % 2
            t1 = s_h[:, par, :]
            t2 = s_u[:, par, :]
            tok_abs = pl.op("scalar", lambda s, db=db, pz=pz, t1=t1: s.activation(
                out=t1, in_=pz[:, :], func=AF.Abs,
                bias=s_bdt2c[:, db:db + 1]),
                waits=W(tok_zmm, tok_dt.get(db - 2),
                        p.get("w15") if db == 0 else None), inc=SS)
            pl.op("scalar", lambda s, t1=t1, t2=t2: s.activation(
                out=t2, in_=t1, func=AF.Exp, scale=-1.0), inc=SS)
            tok_s = pl.op("scalar", lambda s, t1=t1, t2=t2: s.activation(
                out=t1, in_=t2, func=AF.Ln, bias=1.0), inc=SS)
            pl.op("vector", lambda v, db=db, pz=pz: v.tensor_scalar(
                out=s_dt[:, db], in0=pz[:, :], scalar1=s_bdt2c[:, db:db + 1],
                scalar2=0.0, op0=OP.add, op1=OP.max),
                waits=W(tok_abs, p.get("last_a") if db == 0 else None,
                        ck[0].get("wf_dma") if (c == 0 and db == 0) else None),
                inc=SV)
            pl.op("vector", lambda v, db=db, t1=t1: v.tensor_add(
                out=s_dt[:, db], in0=s_dt[:, db], in1=t1),
                waits=W(tok_s), inc=SV)
            tok_dt[db] = pl.op("vector", lambda v, db=db: v.tensor_scalar(
                out=s_dt[:, db], in0=s_dt[:, db], scalar1=1e-7, scalar2=1e6,
                op0=OP.max, op1=OP.min), inc=SV)
        ck[c]["dt"] = tok_dt

    def P4(c):
        p = ck[c - 1] if c > 0 else {}
        tok_dt = ck[c]["dt"]
        tok_lnx = ck[c]["lnx"]
        tok_bcc = ck[c]["bcc"]
        tok_cc = ck[c]["cc"]
        prev_u15 = None
        prev_w15 = None
        tok_ev = None
        tok_w_all = {}
        tok_aa_last = None
        for g in range(NG):
            d0 = g * DBG
            tok_dd = None
            for j in range(DBG):
                tok_dd = pl.op(
                    "vector",
                    lambda v, j=j, d0=d0: v.tensor_scalar(
                        out=s_ddg[:, j], in0=s_ident01[:],
                        scalar1=s_dvec[:, d0 + j:d0 + j + 1], scalar2=None,
                        op0=OP.mult),
                    waits=W(tok_ld, tok_lg) if (c == 0 and g == 0 and j == 0) else (),
                    inc=SV)
            tok_bb = None
            for sub in range(2):
                ssl = slice(sub * SUB, (sub + 1) * SUB)
                tok_bb = pl.op("tensor", lambda t, ssl=ssl: t.matmul(
                    psC[:, ssl], s_identr[0:N, 0:1].broadcast_to([N, 128]), s_bcc[0:N, ssl],
                    start=True, stop=True),
                    waits=W(tok_bcc, tok_cc, tok_lnx, prev_u15,
                            tok_dt[NDB - 2]) if sub == 0 else (),
                    inc=ST)
                tok_bb = pl.op("tensor", lambda t, ssl=ssl: t.matmul(
                    psD[:, ssl], s_identr[0:N, 0:1].broadcast_to([N, 128]), s_ccc[:, ssl],
                    start=True, stop=True),
                    waits=W(prev_w15, tok_dt[NDB - 1]) if sub == 0 else (), inc=ST)
            tok_dterm = {}
            for j in range(DBG):
                py = psA if j == 0 else psB
                tk = None
                for sub in range(2):
                    ssl = slice(sub * SUB, (sub + 1) * SUB)
                    tk = pl.op("tensor", lambda t, j=j, ssl=ssl, py=py, d0=d0: t.matmul(
                        py[:, ssl], s_ddg[:, j, :], s_lnx[:, d0 + j, ssl],
                        start=True, stop=False),
                        waits=W(tok_dd, tok_lnx,
                                tok_ev if g > 0 else p.get("x2_done"))
                        if sub == 0 else (),
                        inc=ST)
                tok_dterm[j] = tk
            if g == NG - 1:
                ck[c]["dterm3"] = tok_dterm[DBG - 1]
            tok_u = {}
            tok_w = {}
            tok_sg_fin = {}
            tok_bbt = {0: tok_bb}
            tok_cbt = {0: tok_bb}
            tok_sgm = {}
            for n in range(N):
                par = n % 2
                q = g * N + n
                wa = (W(tok_dt[d0], tok_dt[d0 + 1]) +
                      (W(p.get("ffnmm")) if g == 0 else ())) if n == 0 else ()
                wa2 = W(tok_w_all.get(q - 2))
                tok_a = None
                for j in range(DBG):
                    tok_a = pl.op("scalar", lambda s, j=j, n=n, par=par, d0=d0: s.activation(
                        out=s_a[par][:, j], in_=s_dt[:, d0 + j],
                        func=AF.Exp, scale=s_amat[:, d0 + j, n:n + 1]),
                        waits=wa + wa2 if j == 0 else (), inc=SS)
                bcb = psC[:].unsqueeze(1).broadcast_to([128, DBG, TC])
                ccb = psD[:].unsqueeze(1).broadcast_to([128, DBG, TC])
                tok_u[n] = pl.op("vector", lambda v, d0=d0, bcb=bcb: v.tensor_mul(
                    out=s_u[:], in0=s_lnx[:, d0:d0 + DBG, :], in1=bcb),
                    waits=W(tok_bbt[n], tok_a,
                            p.get("outlast") if (g == 0 and n == 0) else None),
                    inc=SV)
                pl.op("vector", lambda v, par=par: v.scalar_tensor_tensor(
                    out=s_u[:], in0=s_a[par][:], scalar=1.0, in1=s_u[:],
                    op0=OP.subtract, op1=OP.mult))
                for j in range(DBG):
                    pl.op("vector", lambda v, j=j, n=n, par=par, d0=d0: v.tensor_tensor_scan(
                        out=s_h[:, j], data0=s_a[par][:, j], data1=s_u[:, j],
                        initial=s_carry[:, n, d0 + j:d0 + j + 1],
                        op0=OP.mult, op1=OP.add))
                for j in range(DBG):
                    pl.op("vector", lambda v, n=n, d0=d0, j=j: v.tensor_copy(
                        out=s_carry[:, n, d0 + j:d0 + j + 1],
                        in_=s_h[:, j, TC - 1:TC]))
                for j in range(DBG):
                    tok_w[n] = pl.op("vector", lambda v, j=j, n=n, d0=d0, ccb=ccb: v.scalar_tensor_tensor(
                        out=s_w[:, j], in0=s_h[:, j],
                        scalar=s_inva[:, d0 + j, n:n + 1], in1=psD[:, :],
                        op0=OP.mult, op1=OP.mult),
                        waits=W(tok_cbt[n], tok_sgm.get(n - 1)) if j == 0 else (),
                        inc=SV)
                tok_w_all[q] = tok_w[n]
                if n + 1 < N:
                    for sub in range(2):
                        ssl = slice(sub * SUB, (sub + 1) * SUB)
                        tok_bbt[n + 1] = pl.op("tensor", lambda t, n=n, ssl=ssl: t.matmul(
                            psC[:, ssl],
                            s_identr[0:N, n + 1:n + 2].broadcast_to([N, 128]),
                            s_bcc[0:N, ssl], start=True, stop=True),
                            waits=W(tok_u[n]) if sub == 0 else (), inc=ST)
                tok_sg = None
                for j in range(DBG):
                    py = psA if j == 0 else psB
                    for sub in range(2):
                        ssl = slice(sub * SUB, (sub + 1) * SUB)
                        tok_sg = pl.op("tensor", lambda t, j=j, n=n, ssl=ssl, py=py: t.matmul(
                            py[:, ssl], s_identr[:], s_w[:, j, ssl],
                            start=False, stop=(n == N - 1)),
                            waits=W(tok_w[n], tok_dterm[j]) if (j == 0 and sub == 0) else
                            (W(tok_dterm[j]) if sub == 0 and n == 0 else ()),
                            inc=ST)
                        if n == N - 1:
                            tok_sg_fin[(j, sub)] = tok_sg
                tok_sgm[n] = tok_sg
                if n + 1 < N:
                    for sub in range(2):
                        ssl = slice(sub * SUB, (sub + 1) * SUB)
                        tok_cbt[n + 1] = pl.op("tensor", lambda t, n=n, ssl=ssl: t.matmul(
                            psD[:, ssl],
                            s_identr[0:N, n + 1:n + 2].broadcast_to([N, 128]),
                            s_ccc[:, ssl], start=True, stop=True), inc=ST)
                tok_aa_last = tok_a
            for j in range(DBG):
                py = psA if j == 0 else psB
                for sub in range(2):
                    ssl = slice(sub * SUB, (sub + 1) * SUB)
                    tok_ev = pl.op("vector", lambda v, j=j, py=py, d0=d0, ssl=ssl: v.scalar_tensor_tensor(
                        out=s_xbuf[:, d0 + j, ssl], in0=py[:, ssl], scalar=0.0,
                        in1=s_xbuf[:, d0 + j, ssl], op0=OP.add, op1=OP.add),
                        waits=W(tok_sg_fin[(j, sub)],
                                ck[c]["x"] if g == 0 and j == 0 and sub == 0 else None),
                        inc=SV)
                    ck[c].setdefault("evp2", {})[(d0 + j, sub)] = tok_ev
                ck[c].setdefault("evp", {})[d0 + j] = tok_ev
            for j in range(DBG):
                for sub in range(2):
                    ssl = slice(sub * SUB, (sub + 1) * SUB)
                    ck[c].setdefault("xcp", {})[d0 + j] = pl.dma(
                        "gpsimd", s_lnx[:, d0 + j, ssl], s_xbuf[:, d0 + j, ssl],
                        SXC, waits=W(ck[c]["evp2"][(d0 + j, sub)]))
            prev_u15 = tok_u[N - 1]
            prev_w15 = tok_w[N - 1]
        ck[c]["ev"] = tok_ev
        ck[c]["last_a"] = tok_aa_last
        ck[c]["u15"] = prev_u15
        ck[c]["w15"] = prev_w15

    def P5head(c):
        tok_ev = ck[c]["ev"]
        tok_xcp_last = ck[c]["xcp"][NDB - 1]
        tok_sq2 = {}
        tok_x2r_read = None
        tok_sqS = None
        for sub in range(2):
            ssl = slice(sub * SUB, (sub + 1) * SUB)
            for db in range(NDB):
                par = db % 2
                prev = tok_sq2.get(sub * NDB + db - 2)
                tok_sqS = pl.op("vector", lambda v, db=db, ssl=ssl, par=par: v.tensor_mul(
                    out=s_tsq[:, 2 + par], in0=s_xbuf[:, db, ssl],
                    in1=s_xbuf[:, db, ssl]),
                    waits=W(prev), inc=SV)
                tok_x2r_read = pl.op(
                    "tensor", lambda t, db=db, sub=sub, ssl=ssl: t.matmul(
                        psA[0:1, sub * SUB:(sub + 1) * SUB], s_onesc[:, 0:1],
                        s_lnx[:, db, ssl],
                        start=(db == 0), stop=(db == NDB - 1)),
                    waits=W(tok_xcp_last, tok_ev) if (sub == 0 and db == 0)
                    else (), inc=ST)
                tok_sq2[sub * NDB + db] = pl.op("tensor", lambda t, db=db, sub=sub, par=par: t.matmul(
                    psB[0:1, sub * SUB:(sub + 1) * SUB], s_onesc[:, 0:1],
                    s_tsq[:, 2 + par], start=(db == 0), stop=(db == NDB - 1)),
                    waits=W(tok_sqS), inc=ST)
        ck[c]["x2r_read"] = tok_sq2[2 * NDB - 1]
        ck[c]["sqS"] = tok_sqS
        tok_r = pl.op("vector", lambda v: v.tensor_scalar(
            out=row_m, in0=psA[0:1, :], scalar1=1.0 / DM, scalar2=None,
            op0=OP.mult), waits=W(tok_x2r_read, tok_sq2[2 * NDB - 1]), inc=SV)
        tok_r = pl.op("vector", lambda v: v.tensor_scalar(
            out=row_e2, in0=psB[0:1, :], scalar1=1.0 / DM, scalar2=None,
            op0=OP.mult), inc=SV)
        tok_msq = pl.op("scalar", lambda s: s.activation(
            out=row_msq, in_=row_m, func=AF.Square),
            waits=W(tok_r), inc=SS)
        tok_var = pl.op("vector", lambda v: v.tensor_sub(
            out=row_var, in0=row_e2, in1=row_msq),
            waits=W(tok_msq), inc=SV)
        pl.op("scalar", lambda s: s.activation(
            out=row_msq, in_=row_var, func=AF.Ln, bias=s_epsc[0:1, :]),
            waits=W(tok_var), inc=SS)
        tok_rstd = pl.op("scalar", lambda s: s.activation(
            out=s_rstd[:], in_=row_msq, func=AF.Exp, scale=-0.5), inc=SS)
        tok_mrn = pl.op("vector", lambda v: v.scalar_tensor_tensor(
            out=s_mrn[:], in0=row_m, scalar=-1.0, in1=s_rstd[:],
            op0=OP.mult, op1=OP.mult),
            waits=W(tok_rstd, ck[c]["u15"]), inc=SV)
        tok_bc1 = None
        for sub in range(2):
            ssl = slice(sub * SUB, (sub + 1) * SUB)
            tok_bc1 = pl.op("tensor", lambda t, ssl=ssl: t.matmul(
                psC[:, ssl], s_onesr[:], s_rstd[:, ssl], start=True, stop=True),
                waits=W(tok_rstd, tok_mrn) if sub == 0 else (), inc=ST)
            tok_bc1 = pl.op("tensor", lambda t, ssl=ssl: t.matmul(
                psD[:, ssl], s_onesr[:], s_mrn[:, ssl], start=True, stop=True),
                inc=ST)
        ck[c]["bc15"] = tok_bc1
        tok_mb = pl.op("vector", lambda v: v.tensor_copy(out=s_mbrb[:, 0], in_=psC[:, :]),
                       waits=W(tok_bc1, ck[c]["w15"]), inc=SV)
        tok_mb = pl.op("vector", lambda v: v.tensor_copy(out=s_mbrb[:, 1], in_=psD[:, :]),
                       inc=SV)
        ck[c]["mb5"] = tok_mb
        tok_ln2 = None
        for hf in range(2):
            tok_ln2 = pl.op("vector", lambda v, hf=hf: v.tensor_mul(
                out=l2b[hf][:], in0=s_xbuf[:, 4 * hf:4 * hf + 4, :], in1=rb4),
                waits=W(tok_mb, ck[c]["last_a"]) if hf == 0 else (), inc=SV)
            tok_ln2 = pl.op("vector", lambda v, hf=hf: v.tensor_add(
                out=l2b[hf][:], in0=l2b[hf][:], in1=mrn4), inc=SV)
        ck[c]["ln2"] = tok_ln2

    def P5tail(c):
        p = ck[c - 1] if c > 0 else {}
        tok_ln2 = ck[c]["ln2"]
        s_weff = [s_wslab[:, 0], s_wslab[:, 1]]
        tok_wob = {}
        tok_wob[0] = pl.dma("sync", s_weff[0][:], d_weff[:, 0, :, :], SDS[0],
                            waits=W(ck[c]["cc"], ck[c]["bc15"], tok_ln2,
                                    ck[0]["wf_dma"] if c == 0 else None))
        tok_ffn_ev = None
        tok_ev_par = {}
        tok_ffn_mm = {}
        for ob in range(NDB):
            par = ob % 2
            if ob + 1 < NDB:
                tok_wob[ob + 1] = pl.dma(
                    "sync", s_weff[(ob + 1) % 2][:], d_weff[:, ob + 1, :, :],
                    SDS[(ob + 1) % 2],
                    waits=W(tok_ffn_mm.get(ob - 1), tok_ln2 if ob == 0 else None))
            for sub in range(2):
                ssl = slice(sub * SUB, (sub + 1) * SUB)
                gidx = 2 * ob + sub
                pf = psA if gidx % 2 == 0 else psB
                tok_mm = None
                for kb in range(NDB):
                    tok_mm = pl.op("tensor", lambda t, ob=ob, kb=kb, ssl=ssl, pf=pf, par=par: t.matmul(
                        pf[:, 0:SUB], s_weff[par][:, kb, :],
                        l2b[kb // 4][:, kb % 4, ssl],
                        start=(kb == 0), stop=(kb == NDB - 1)),
                        waits=W(tok_ln2, tok_wob[ob],
                                tok_ev_par.get(gidx - 2) if gidx >= 2
                                else ck[c]["x2r_read"])
                        if kb == 0 else (),
                        inc=ST)
                tok_ffn_mm[ob] = tok_mm
                slot = (s_h if ob % 4 < 2 else s_u)[:, ob % 2, :]
                tok_ffn_ev = pl.op("vector", lambda v, ob=ob, ssl=ssl, pf=pf, slot=slot: v.scalar_tensor_tensor(
                    out=slot[:, ssl], in0=pf[:, 0:SUB],
                    scalar=s_bffn[:, ob:ob + 1], in1=s_xbuf[:, ob, ssl],
                    op0=OP.add, op1=OP.add),
                    waits=W(tok_mm, ck[c].get("out%d" % (ob - 4))), inc=SV)
                tok_ev_par[gidx] = tok_ffn_ev
            ck[c]["out%d" % ob] = pl.dma(
                "sync", out_v[:, ob, c * TC:(c + 1) * TC], slot, SO,
                waits=W(tok_ffn_ev))
        ck[c]["outlast"] = ck[c]["out%d" % (NDB - 1)]
        ck[c]["x2_done"] = tok_ffn_ev
        ck[c]["ffnmm"] = tok_ffn_mm[NDB - 1]

    P0_xr(0)
    P1(0)
    P2(0)
    LNX(0)
    P3(0)
    P0_x(0)
    for c in range(NCH):
        P4(c)
        P5head(c)
        if c + 1 < NCH:
            P0_xr(c + 1)
            P1(c + 1)
            LNX(c + 1)
            P2(c + 1)
            P3(c + 1)
        P5tail(c)
        if c + 1 < NCH:
            P0_x(c + 1)

    n_out = pl.cnt[SO]

    sem_names = [SDW, SGW, SWT, SDF, SDX, SGX, SDC, SXC, SDS[0], SDS[1],
                 ST, SS, SV, SO]
    with es, nc.Block() as block:
        sems = {nm: es.enter_context(nc.semaphore("sem_" + nm)) for nm in sem_names}

        def replay(eng, name):
            for waits, emit, inc, inc_amt in pl.streams[name]:
                for (s, v) in waits:
                    eng.wait_ge(sems[s], v)
                inst = emit(eng)
                if inc is not None:
                    inst.then_inc(sems[inc], inc_amt)

        @block.sync
        def _(sync):
            replay(sync, "sync")
            sync.wait_ge(sems[SO], n_out)

        @block.gpsimd
        def _(gpsimd):
            replay(gpsimd, "gpsimd")

        @block.scalar
        def _(scalar):
            replay(scalar, "scalar")

        @block.vector
        def _(vector):
            replay(vector, "vector")

        @block.tensor
        def _(tensor):
            replay(tensor, "tensor")

    return nc


_NC_CACHE = {}


def kernel(**inputs):
    x = np.asarray(inputs["L0_attn_x"], dtype=np.float32)
    A2 = np.ascontiguousarray(np.asarray(inputs["L0_attn_A"], np.float32).reshape(DM, N))
    common = {
        "w_in": np.ascontiguousarray(np.asarray(inputs["L0_attn_W_in"], np.float32)),
        "b_in_c": np.ascontiguousarray(np.asarray(inputs["L0_attn_b_in"], np.float32).reshape(2 * N, 1)),
        "w_dt1": np.ascontiguousarray(np.asarray(inputs["L0_attn_W_dt1"], np.float32)),
        "b_dt1_c": np.ascontiguousarray(np.asarray(inputs["L0_attn_b_dt1"], np.float32).reshape(RDT, 1)),
        "w_dt2": np.ascontiguousarray(np.asarray(inputs["L0_attn_W_dt2"], np.float32)),
        "b_dt2_c": np.ascontiguousarray(np.asarray(inputs["L0_attn_b_dt2"], np.float32).reshape(NDB, 128).T),
        "a_mat": A2,
        "d_vec": np.ascontiguousarray(np.asarray(inputs["L0_attn_D"], np.float32).reshape(NDB, 128).T),
        "wt_ffn": np.ascontiguousarray(np.asarray(inputs["L0_ffn_W"], np.float32).T),
        "b_ffn_c": np.ascontiguousarray(np.asarray(inputs["L0_ffn_b"], np.float32).reshape(NDB, 128).T),
        "lora_down_m": np.ascontiguousarray(np.asarray(inputs["L0_ffn_lora_down"], np.float32)),
        "lora_up_m": np.ascontiguousarray(np.asarray(inputs["L0_ffn_lora_up"], np.float32)),
        "up_aux_t": np.ascontiguousarray(np.asarray(inputs["L0_ffn_up_aux"], np.float32).T),
        "down_aux_m": np.ascontiguousarray(np.asarray(inputs["L0_ffn_down_aux"], np.float32)),
        "ident01": np.eye(128, dtype=np.float32),
        "ones_row": np.ones((1, 128), np.float32),
        "ones_col": np.ones((128, 8), np.float32),
        "eps_col": np.full((128, 1), EPS, np.float32),
    }
    in_maps = []
    for c in range(NCORE):
        m = dict(common)
        m["x_t"] = np.ascontiguousarray(x[c].T)
        in_maps.append(m)

    if "nc" not in _NC_CACHE:
        _NC_CACHE["nc"] = build_program()
    nc = _NC_CACHE["nc"]
    res = run_bass_kernel_spmd(nc, in_maps, core_ids=list(range(NCORE)))
    out = np.stack([np.asarray(res.results[c]["out_t"]).T for c in range(NCORE)], axis=0)
    return out


# version 21
# speedup vs baseline: 1.0075x; 1.0075x over previous
import numpy as np
from contextlib import ExitStack

import concourse.bass as bass
import concourse.mybir as mybir
from concourse.bass_utils import run_bass_kernel_spmd

F32 = mybir.dt.float32
F32R = mybir.dt.float32r
BF16 = mybir.dt.bfloat16
AF = mybir.ActivationFunctionType
OP = mybir.AluOpType

NCORE = 8
L = 4096
DM = 1024
N = 16
RDT = 64
NDB = 8
TC = 1024
NCH = L // TC
SUB = 512
DBG = 2
NG = NDB // DBG
EPS = 1e-5


class Plan:

    ENGINES = ("sync", "gpsimd", "scalar", "vector", "tensor")

    def __init__(self):
        self.streams = {e: [] for e in self.ENGINES}
        self.cnt = {}

    def new_sem(self, name):
        self.cnt[name] = 0
        return name

    def op(self, eng, emit, waits=(), inc=None, inc_amt=1):
        tok = None
        if inc is not None:
            self.cnt[inc] += inc_amt
            tok = (inc, self.cnt[inc])
        self.streams[eng].append((tuple(waits), emit, inc, inc_amt))
        return tok

    def dma(self, eng, out, in_, sem, waits=()):
        def emit(e, out=out, in_=in_):
            return e.dma_start(out=out, in_=in_)
        return self.op(eng, emit, waits=waits, inc=sem, inc_amt=16)


def build_program(debug=False):
    nc = bass.Bass("TRN2", target_bir_lowering=False,
                   detect_race_conditions=False)

    d_x = nc.dram_tensor("x_t", [DM, L], F32, kind="ExternalInput").ap()
    d_win = nc.dram_tensor("w_in", [DM, 2 * N], F32, kind="ExternalInput").ap()
    d_bin = nc.dram_tensor("b_in_c", [2 * N, 1], F32, kind="ExternalInput").ap()
    d_wdt1 = nc.dram_tensor("w_dt1", [DM, RDT], F32, kind="ExternalInput").ap()
    d_bdt1 = nc.dram_tensor("b_dt1_c", [RDT, 1], F32, kind="ExternalInput").ap()
    d_wdt2 = nc.dram_tensor("w_dt2", [RDT, DM], F32, kind="ExternalInput").ap()
    d_bdt2 = nc.dram_tensor("b_dt2_c", [128, NDB], F32, kind="ExternalInput").ap()
    d_amat = nc.dram_tensor("a_mat", [DM, N], F32, kind="ExternalInput").ap()
    d_dvec = nc.dram_tensor("d_vec", [128, NDB], F32, kind="ExternalInput").ap()
    d_wffn = nc.dram_tensor("wt_ffn", [DM, DM], F32, kind="ExternalInput").ap()
    d_bffn = nc.dram_tensor("b_ffn_c", [128, NDB], F32, kind="ExternalInput").ap()
    d_ldown = nc.dram_tensor("lora_down_m", [4, 100], F32, kind="ExternalInput").ap()
    d_lup = nc.dram_tensor("lora_up_m", [50, 4], F32, kind="ExternalInput").ap()
    d_uaxt = nc.dram_tensor("up_aux_t", [50, DM], F32, kind="ExternalInput").ap()
    d_daux = nc.dram_tensor("down_aux_m", [100, DM], F32, kind="ExternalInput").ap()
    d_ident = nc.dram_tensor("ident01", [128, 128], F32, kind="ExternalInput").ap()
    d_onesr = nc.dram_tensor("ones_row", [1, 128], F32, kind="ExternalInput").ap()
    d_onesc = nc.dram_tensor("ones_col", [128, 8], F32, kind="ExternalInput").ap()
    d_eps = nc.dram_tensor("eps_col", [128, 1], F32, kind="ExternalInput").ap()
    d_out = nc.dram_tensor("out_t", [DM, L], F32, kind="ExternalOutput").ap()
    d_weff = nc.dram_tensor("weff_sc", [128, NDB, NDB, 128], BF16, kind="Internal").ap()

    x_v = d_x.rearrange("(db p) l -> p db l", p=128)
    out_v = d_out.rearrange("(db p) l -> p db l", p=128)
    amat_v = d_amat.rearrange("(db p) n -> p db n", p=128)
    win_v = d_win.rearrange("(db p) m -> p db m", p=128)
    wdt1_v = d_wdt1.rearrange("(db p) m -> p db m", p=128)
    wdt2_v = d_wdt2.rearrange("r (db p) -> r db p", p=128)
    wffn_v = d_wffn.rearrange("(kb p) o -> p kb o", p=128)

    es = ExitStack()
    sb = lambda name, shape, dt: es.enter_context(nc.sbuf_tensor(name, shape, dt))
    psb = lambda name, shape: es.enter_context(nc.psum_tensor(name, shape, F32))

    s_xbuf = sb("xbuf", [128, NDB, TC], F32)
    s_xr = sb("xr", [128, NDB, TC], F32R)
    s_lnx = sb("lnx", [128, NDB, TC], F32R)
    s_dt = sb("dtout", [128, NDB, TC], F32)
    s_a = [sb(f"a{i}", [128, DBG, TC], F32) for i in range(2)]
    s_u = sb("u", [128, DBG, TC], F32)
    s_h = sb("hh", [128, DBG, TC], F32)
    s_w = sb("ww", [128, DBG, TC], F32R)
    s_tsq = sb("tsq", [128, 4, SUB], F32R)
    s_bcc = sb("bcc", [2 * N, TC], F32R)
    s_z1 = sb("z1c", [RDT, TC], F32R)
    s_ccc = sb("ccc", [N, TC], F32R)
    s_winr = sb("winr", [128, NDB, 2 * N], F32R)
    s_wdt1r = sb("wdt1r", [128, NDB, RDT], F32R)
    s_wdt2r = sb("wdt2r", [RDT, NDB, 128], F32R)
    s_small = sb("smallv", [128, 4 + 3 * NDB], F32)
    s_binc = s_small[0:2 * N, 0:1]
    s_bdt1c = s_small[0:RDT, 1:2]
    s_wsin = s_small[0:2 * N, 2:3]
    s_wsdt1 = s_small[0:RDT, 3:4]
    s_bdt2c = s_small[:, 4:4 + NDB]
    s_dvec = s_small[:, 4 + NDB:4 + 2 * NDB]
    s_bffn = s_small[:, 4 + 2 * NDB:4 + 3 * NDB]
    s_amat = sb("amat", [128, NDB, N], F32)
    s_inva = sb("inva", [128, NDB, N], F32)
    s_identr = sb("identr", [128, 128], F32R)
    s_ident01 = s_identr
    s_onesr = sb("onesrr", [1, 128], F32R)
    s_onesc = sb("onescr", [128, 8], F32R)
    s_carry = sb("carry", [128, N, NDB], F32)
    s_epsc = sb("epsc", [128, 1], F32)
    s_ddg = sb("ddg", [128, DBG, 128], F32R)
    s_wslab = sb("wslab", [128, 2, NDB, 128], BF16)

    s_mbrb = s_w[:]
    s_l1 = s_lnx[0:4, 0, 0:DM]
    s_l2 = s_lnx[0:100, 1, 0:DM]
    s_wtmp = [s_xbuf[:, 6, :], s_xbuf[:, 7, :]]
    s_lup = s_lnx[0:50, 2, 0:4]
    s_ldn = s_lnx[0:4, 3, 0:100]
    s_uax = s_w[0:50, 0, :]
    s_dax = s_lnx[0:100, 6, 0:DM]

    row_m = s_u[0:1, 0, :]
    row_e2 = s_u[0:1, 1, :]
    s_sp1 = s_h[:, 1, :]
    row_msq = s_sp1[0:1, 0:TC]
    row_var = row_e2
    s_sp2 = s_u[:, 0, :]
    s_rstd = s_tsq[0:1, 0:2, :].rearrange("p a b -> p (a b)")
    s_mrn = s_ccc[0:1, :]

    psA = psb("psA", [128, TC])
    psB = psb("psB", [128, TC])
    psC = psb("psC", [128, TC])
    psD = psb("psD", [128, TC])

    pl = Plan()
    SDW = pl.new_sem("sdw")
    SGW = pl.new_sem("sgw")
    SWT = pl.new_sem("swt")
    SDF = pl.new_sem("sdf")
    SDX = pl.new_sem("sdx")
    SGX = pl.new_sem("sgx")
    SDC = pl.new_sem("sdc")
    SXC = pl.new_sem("sxc")
    SDS = [pl.new_sem("sds0"), pl.new_sem("sds1")]
    ST = pl.new_sem("st_")
    SS = pl.new_sem("ss")
    SV = pl.new_sem("sv")
    SO = pl.new_sem("so")

    W = lambda *toks: tuple(t for t in toks if t is not None)

    t_ld = []
    t_ld.append(pl.dma("sync", s_binc[:], d_bin, SDW))
    t_ld.append(pl.dma("sync", s_bdt1c[:], d_bdt1, SDW))
    t_ld.append(pl.dma("sync", s_bdt2c[:], d_bdt2, SDW))
    t_ld.append(pl.dma("sync", s_amat[:], amat_v, SDW))
    t_ld.append(pl.dma("sync", s_dvec[:], d_dvec, SDW))
    t_ld.append(pl.dma("sync", s_bffn[:], d_bffn, SDW))
    t_ld.append(pl.dma("sync", s_epsc[:], d_eps, SDW))
    t_lg = []
    t_lg.append(pl.dma("gpsimd", s_winr[:], win_v, SGW))
    t_lg.append(pl.dma("gpsimd", s_wdt1r[:], wdt1_v, SGW))
    t_lg.append(pl.dma("gpsimd", s_wdt2r[:], wdt2_v, SGW))
    t_lg.append(pl.dma("gpsimd", s_identr[:], d_ident, SGW))
    t_lg.append(pl.dma("gpsimd", s_onesr[:], d_onesr, SGW))
    t_lg.append(pl.dma("gpsimd", s_onesc[:], d_onesc, SGW))
    t_lg.append(pl.dma("gpsimd", s_lup[:], d_lup, SGW))
    t_lg.append(pl.dma("gpsimd", s_ldn[:], d_ldown, SGW))
    t_lg.append(pl.dma("gpsimd", s_uax[:], d_uaxt, SGW))
    t_lg.append(pl.dma("gpsimd", s_dax[:], d_daux, SGW))
    tok_ld = t_ld[-1]
    tok_lg = tok_lg2 = t_lg[-1]
    tok_xr0 = pl.dma("gpsimd", s_xr[:], x_v[:, :, 0:TC], SGX)

    tok_carry0 = pl.op("vector", lambda v: v.memset(s_carry[:], 0.0), inc=SV)
    tok_inva = pl.op(
        "vector",
        lambda v: v.reciprocal(
            out=s_inva[:].rearrange("p a b -> p (a b)"),
            in_=s_amat[:].rearrange("p a b -> p (a b)")),
        waits=W(tok_ld), inc=SV)
    for db in range(NDB):
        pl.op("tensor",
              lambda t, db=db: t.matmul(psA[0:2 * N, 0:8], s_winr[:, db], s_onesc[:],
                                        start=(db == 0), stop=(db == NDB - 1)),
              waits=W(tok_lg) if db == 0 else ())
    tok_t = pl.op("tensor",
                  lambda t: t.matmul(psB[0:RDT, 0:8], s_wdt1r[:, 0], s_onesc[:],
                                     start=True, stop=False),
                  inc=ST)
    for db in range(1, NDB):
        tok_t = pl.op("tensor",
                      lambda t, db=db: t.matmul(psB[0:RDT, 0:8], s_wdt1r[:, db],
                                                s_onesc[:], start=False,
                                                stop=(db == NDB - 1)),
                      inc=ST)
    tok_ws = pl.op("vector", lambda v: v.tensor_copy(out=s_wsin[:], in_=psA[0:2 * N, 0:1]),
                   waits=W(tok_t), inc=SV)
    tok_ws = pl.op("vector", lambda v: v.tensor_copy(out=s_wsdt1[:], in_=psB[0:RDT, 0:1]),
                   inc=SV)

    tok_t = None
    for sub in range(2):
        tok_t = pl.op("tensor",
                      lambda t, sub=sub: t.matmul(
                          psA[0:4, sub * SUB:(sub + 1) * SUB], s_lup[:],
                          s_uax[:, sub * SUB:(sub + 1) * SUB], start=True, stop=True),
                      waits=W(tok_lg2, tok_ws), inc=ST)
    tok_v = pl.op("vector", lambda v: v.tensor_copy(out=s_l1[:], in_=psA[0:4, :]),
                  waits=W(tok_t), inc=SV)
    tok_t = None
    for sub in range(2):
        tok_t = pl.op("tensor",
                      lambda t, sub=sub: t.matmul(
                          psA[0:100, sub * SUB:(sub + 1) * SUB], s_ldn[:],
                          s_l1[:, sub * SUB:(sub + 1) * SUB], start=True, stop=True),
                      waits=W(tok_v), inc=ST)
    tok_v = pl.op("vector", lambda v: v.tensor_copy(out=s_l2[:], in_=psA[0:100, :]),
                  waits=W(tok_t), inc=SV)
    tok_wf_dma = None
    wtmpb = [s_dt[:, 0].bitcast(BF16)[:, 0:DM],
             s_dt[:, 0].bitcast(BF16)[:, DM:2 * DM]]
    tok_vw_k = {}
    tok_wf_k = {}
    for kb in range(NDB):
        par = kb % 2
        pdl = psA if par else psB
        tok_g = pl.dma("sync", s_wtmp[par][:], wffn_v[:, kb, :], SWT,
                       waits=W(tok_vw_k.get(kb - 2)))
        tok_t = None
        for sub in range(2):
            tok_t = pl.op("tensor",
                          lambda t, kb=kb, sub=sub, pdl=pdl: t.matmul(
                              pdl[:, sub * SUB:(sub + 1) * SUB],
                              s_dax[:, kb * 128:(kb + 1) * 128],
                              s_l2[:, sub * SUB:(sub + 1) * SUB],
                              start=True, stop=True),
                          waits=W(tok_v) if (kb <= 1 and sub == 0) else
                          (W(tok_vw_k.get(kb - 2)) if sub == 0 else ()),
                          inc=ST)
        tok_vw_k[kb] = pl.op("vector",
                             lambda v, kb=kb, par=par, pdl=pdl: v.tensor_add(
                                 out=wtmpb[par], in0=s_wtmp[par][:],
                                 in1=pdl[:, :]),
                             waits=W(tok_t, tok_g, tok_wf_k.get(kb - 2)), inc=SV)
        tok_wf_k[kb] = tok_wf_dma = pl.dma(
            "sync", d_weff[:, :, kb, :],
            wtmpb[par].rearrange("p (ob ow) -> p ob ow", ow=128),
            SDF, waits=W(tok_vw_k[kb]))
    tok_wfold_v = tok_vw_k[NDB - 1]

    rb_b = s_mbrb[:, 0].unsqueeze(1).broadcast_to([128, NDB, TC])
    mrn_b = s_mbrb[:, 1].unsqueeze(1).broadcast_to([128, NDB, TC])
    rb4 = s_mbrb[:, 0].unsqueeze(1).broadcast_to([128, 4, TC])
    mrn4 = s_mbrb[:, 1].unsqueeze(1).broadcast_to([128, 4, TC])
    l2b = [s_a[i][:].bitcast(BF16).rearrange("p a b -> p (a b)")
           .rearrange("p (a b) -> p a b", b=TC) for i in range(2)]

    ck = [dict() for _ in range(NCH + 1)]
    ck[0]["wf_dma"] = tok_wf_dma

    def P0_xr(c):
        if c == 0:
            ck[0]["xr"] = tok_xr0
            return
        p = ck[c - 1]
        ck[c]["xr"] = pl.dma(
            "gpsimd", s_xr[:], x_v[:, :, c * TC:(c + 1) * TC], SGX,
            waits=W(p.get("lnx"), p.get("bcraw"), p.get("stats")))

    def P0_x(c):
        p = ck[c - 1] if c > 0 else {}
        ck[c]["x"] = pl.dma(
            "sync", s_xbuf[:], x_v[:, :, c * TC:(c + 1) * TC], SDX,
            waits=W(p.get("x2_done"), p.get("sqS"), p.get("x2r_read"),
                    p.get("ln2"), tok_wfold_v if c == 0 else None))

    def P1(c):
        p = ck[c - 1] if c > 0 else {}
        tok_xr = ck[c]["xr"]
        tok_sq = {}
        tok_stats = None
        for sub in range(2):
            ssl = slice(sub * SUB, (sub + 1) * SUB)
            for db in range(NDB):
                par = db % 2
                prev = tok_sq.get(sub * NDB + db - 2)
                tok_s = pl.op(
                    "scalar",
                    lambda s, db=db, ssl=ssl, par=par: s.activation(
                        out=s_tsq[:, par], in_=s_xr[:, db, ssl], func=AF.Square),
                    waits=W(tok_xr, prev,
                            p.get("bc15") if (sub == 0 and db == 0) else None,
                            tok_ld if (c == 0 and sub == 0 and db == 0) else None),
                    inc=SS)
                tok_sq[sub * NDB + db] = pl.op(
                    "tensor",
                    lambda t, db=db, sub=sub, par=par: t.matmul(
                        psD[0:1, sub * SUB:(sub + 1) * SUB], s_onesc[:, 0:1],
                        s_tsq[:, par], start=(db == 0), stop=(db == NDB - 1)),
                    waits=W(tok_s, p.get("mb5") if db == 0 else None), inc=ST)
                tok_stats = pl.op(
                    "tensor",
                    lambda t, db=db, sub=sub, ssl=ssl: t.matmul(
                        psC[0:1, sub * SUB:(sub + 1) * SUB], s_onesc[:, 0:1],
                        s_xr[:, db, ssl], start=(db == 0), stop=(db == NDB - 1)),
                    waits=W(tok_xr, p.get("mb5")) if db == 0 else (), inc=ST)
        ck[c]["stats"] = tok_stats
        tok_r = pl.op("vector", lambda v: v.tensor_scalar(
            out=row_m, in0=psC[0:1, :], scalar1=1.0 / DM, scalar2=None,
            op0=OP.mult), waits=W(tok_stats, tok_sq[2 * NDB - 1]), inc=SV)
        tok_r = pl.op("vector", lambda v: v.tensor_scalar(
            out=row_e2, in0=psD[0:1, :], scalar1=1.0 / DM, scalar2=None,
            op0=OP.mult), inc=SV)
        tok_msq = pl.op("scalar", lambda s: s.activation(
            out=row_msq, in_=row_m, func=AF.Square),
            waits=W(tok_r), inc=SS)
        tok_var = pl.op("vector", lambda v: v.tensor_sub(
            out=row_var, in0=row_e2, in1=row_msq),
            waits=W(tok_msq), inc=SV)
        pl.op("scalar", lambda s: s.activation(
            out=row_msq, in_=row_var, func=AF.Ln, bias=s_epsc[0:1, :]),
            waits=W(tok_var), inc=SS)
        tok_rstd = pl.op("scalar", lambda s: s.activation(
            out=s_rstd[:], in_=row_msq, func=AF.Exp, scale=-0.5), inc=SS)
        tok_mrn = pl.op("vector", lambda v: v.scalar_tensor_tensor(
            out=s_mrn[:], in0=row_m, scalar=-1.0, in1=s_rstd[:],
            op0=OP.mult, op1=OP.mult), waits=W(tok_rstd), inc=SV)
        tok_bc1 = None
        for sub in range(2):
            ssl = slice(sub * SUB, (sub + 1) * SUB)
            tok_bc1 = pl.op("tensor", lambda t, ssl=ssl: t.matmul(
                psC[:, ssl], s_onesr[:], s_rstd[:, ssl], start=True, stop=True),
                waits=W(tok_rstd, tok_mrn), inc=ST)
            tok_bc1 = pl.op("tensor", lambda t, ssl=ssl: t.matmul(
                psD[:, ssl], s_onesr[:], s_mrn[:, ssl], start=True, stop=True),
                inc=ST)
        ck[c]["bc1"] = tok_bc1
        tok_mb = pl.op("vector", lambda v: v.tensor_copy(out=s_mbrb[:, 0], in_=psC[:, :]),
                       waits=W(tok_bc1), inc=SV)
        tok_mb = pl.op("vector", lambda v: v.tensor_copy(out=s_mbrb[:, 1], in_=psD[:, :]),
                       inc=SV)
        ck[c]["mb"] = tok_mb

    def P2(c):
        tok_mb = ck[c]["mb"]
        tok_bcraw = None
        for sub in range(2):
            ssl = slice(sub * SUB, (sub + 1) * SUB)
            for db in range(NDB):
                tok_bcraw = pl.op("tensor", lambda t, db=db, ssl=ssl: t.matmul(
                    psC[0:2 * N, ssl], s_winr[:, db], s_xr[:, db, ssl],
                    start=(db == 0), stop=(db == NDB - 1)),
                    waits=W(tok_mb if (sub == 0 and db == 0) else None), inc=ST)
                tok_bcraw = pl.op("tensor", lambda t, db=db, ssl=ssl: t.matmul(
                    psD[0:RDT, ssl], s_wdt1r[:, db], s_xr[:, db, ssl],
                    start=(db == 0), stop=(db == NDB - 1)), inc=ST)
        ck[c]["bcraw"] = tok_bcraw
        tok_bcc = pl.op("vector", lambda v: v.tensor_mul(
            out=s_bcc[:], in0=psC[0:2 * N, :], in1=s_mbrb[0:2 * N, 0]),
            waits=W(tok_bcraw, tok_mb), inc=SV)
        tok_bcc = pl.op("vector", lambda v: v.scalar_tensor_tensor(
            out=s_bcc[:], in0=s_mbrb[0:2 * N, 1], scalar=s_wsin[:], in1=s_bcc[:],
            op0=OP.mult, op1=OP.add), inc=SV)
        tok_bcc = pl.op("vector", lambda v: v.tensor_scalar(
            out=s_bcc[:], in0=s_bcc[:], scalar1=s_binc[:], scalar2=None,
            op0=OP.add), inc=SV)
        ck[c]["bcc"] = tok_bcc
        tok_z1 = pl.op("vector", lambda v: v.tensor_mul(
            out=s_z1[:], in0=psD[0:RDT, :], in1=s_mbrb[0:RDT, 0]), inc=SV)
        tok_z1 = pl.op("vector", lambda v: v.scalar_tensor_tensor(
            out=s_z1[:], in0=s_mbrb[0:RDT, 1], scalar=s_wsdt1[:], in1=s_z1[:],
            op0=OP.mult, op1=OP.add), inc=SV)
        tok_z1 = pl.op("vector", lambda v: v.tensor_scalar(
            out=s_z1[:], in0=s_z1[:], scalar1=s_bdt1c[:], scalar2=None,
            op0=OP.add), inc=SV)
        ck[c]["z1"] = tok_z1
        ck[c]["cc"] = pl.dma("sync", s_ccc[:], s_bcc[N:2 * N, :], SDC,
                             waits=W(tok_bcc))

    def LNX(c):
        p = ck[c - 1] if c > 0 else {}
        tok_lnx = pl.op("vector", lambda v: v.tensor_mul(
            out=s_lnx[:], in0=s_xr[:], in1=rb_b),
            waits=W(ck[c]["mb"], p.get("dterm3"), p.get("x2r_read"),
                    ck[c]["xr"]), inc=SV)
        tok_lnx = pl.op("vector", lambda v: v.tensor_add(
            out=s_lnx[:], in0=s_lnx[:], in1=mrn_b), inc=SV)
        ck[c]["lnx"] = tok_lnx

    def P3(c):
        p = ck[c - 1] if c > 0 else {}
        tok_z1 = ck[c]["z1"]
        tok_dt = {}
        for db in range(NDB):
            pz = psC if db % 2 == 0 else psD
            w0 = W(tok_z1, ck[c]["bcc"]) if db < 2 else W(tok_dt.get(db - 2))
            tok_zmm = None
            for sub in range(2):
                ssl = slice(sub * SUB, (sub + 1) * SUB)
                tok_zmm = pl.op("tensor", lambda t, db=db, ssl=ssl, pz=pz: t.matmul(
                    pz[:, ssl], s_wdt2r[:, db, :], s_z1[:, ssl],
                    start=True, stop=True), waits=w0 if sub == 0 else (), inc=ST)
            par = db % 2
            t1 = s_h[:, par, :]
            t2 = s_u[:, par, :]
            tok_abs = pl.op("scalar", lambda s, db=db, pz=pz, t1=t1: s.activation(
                out=t1, in_=pz[:, :], func=AF.Abs,
                bias=s_bdt2c[:, db:db + 1]),
                waits=W(tok_zmm, tok_dt.get(db - 2),
                        p.get("w15") if db == 0 else None), inc=SS)
            pl.op("scalar", lambda s, t1=t1, t2=t2: s.activation(
                out=t2, in_=t1, func=AF.Exp, scale=-1.0), inc=SS)
            tok_s = pl.op("scalar", lambda s, t1=t1, t2=t2: s.activation(
                out=t1, in_=t2, func=AF.Ln, bias=1.0), inc=SS)
            pl.op("vector", lambda v, db=db, pz=pz: v.tensor_scalar(
                out=s_dt[:, db], in0=pz[:, :], scalar1=s_bdt2c[:, db:db + 1],
                scalar2=0.0, op0=OP.add, op1=OP.max),
                waits=W(tok_abs, p.get("last_a") if db == 0 else None,
                        ck[0].get("wf_dma") if (c == 0 and db == 0) else None),
                inc=SV)
            pl.op("vector", lambda v, db=db, t1=t1: v.tensor_add(
                out=s_dt[:, db], in0=s_dt[:, db], in1=t1),
                waits=W(tok_s), inc=SV)
            tok_dt[db] = pl.op("vector", lambda v, db=db: v.tensor_scalar(
                out=s_dt[:, db], in0=s_dt[:, db], scalar1=1e-7, scalar2=1e6,
                op0=OP.max, op1=OP.min), inc=SV)
        ck[c]["dt"] = tok_dt

    def P4(c):
        p = ck[c - 1] if c > 0 else {}
        tok_dt = ck[c]["dt"]
        tok_lnx = ck[c]["lnx"]
        tok_bcc = ck[c]["bcc"]
        tok_cc = ck[c]["cc"]
        prev_u15 = None
        prev_w15 = None
        tok_ev = None
        tok_w_all = {}
        tok_aa_last = None
        for g in range(NG):
            d0 = g * DBG
            tok_dd = None
            for j in range(DBG):
                tok_dd = pl.op(
                    "vector",
                    lambda v, j=j, d0=d0: v.tensor_scalar(
                        out=s_ddg[:, j], in0=s_ident01[:],
                        scalar1=s_dvec[:, d0 + j:d0 + j + 1], scalar2=None,
                        op0=OP.mult),
                    waits=W(tok_ld, tok_lg) if (c == 0 and g == 0 and j == 0) else (),
                    inc=SV)
            tok_bb = None
            for sub in range(2):
                ssl = slice(sub * SUB, (sub + 1) * SUB)
                tok_bb = pl.op("tensor", lambda t, ssl=ssl: t.matmul(
                    psC[:, ssl], s_identr[0:N, 0:1].broadcast_to([N, 128]), s_bcc[0:N, ssl],
                    start=True, stop=True),
                    waits=W(tok_bcc, tok_cc, tok_lnx, prev_u15,
                            tok_dt[NDB - 2]) if sub == 0 else (),
                    inc=ST)
                tok_bb = pl.op("tensor", lambda t, ssl=ssl: t.matmul(
                    psD[:, ssl], s_identr[0:N, 0:1].broadcast_to([N, 128]), s_ccc[:, ssl],
                    start=True, stop=True),
                    waits=W(prev_w15, tok_dt[NDB - 1]) if sub == 0 else (), inc=ST)
            tok_dterm = {}
            for j in range(DBG):
                py = psA if j == 0 else psB
                tk = None
                for sub in range(2):
                    ssl = slice(sub * SUB, (sub + 1) * SUB)
                    tk = pl.op("tensor", lambda t, j=j, ssl=ssl, py=py, d0=d0: t.matmul(
                        py[:, ssl], s_ddg[:, j, :], s_lnx[:, d0 + j, ssl],
                        start=True, stop=False),
                        waits=W(tok_dd, tok_lnx,
                                tok_ev if g > 0 else p.get("x2_done"))
                        if sub == 0 else (),
                        inc=ST)
                tok_dterm[j] = tk
            if g == NG - 1:
                ck[c]["dterm3"] = tok_dterm[DBG - 1]
            tok_u = {}
            tok_w = {}
            tok_sg_fin = {}
            tok_bbt = {0: tok_bb}
            tok_cbt = {0: tok_bb}
            tok_sgm = {}
            for n in range(N):
                par = n % 2
                q = g * N + n
                wa = (W(tok_dt[d0], tok_dt[d0 + 1]) +
                      (W(p.get("ffnmm")) if g == 0 else ())) if n == 0 else ()
                wa2 = W(tok_w_all.get(q - 2))
                tok_a = None
                for j in range(DBG):
                    tok_a = pl.op("scalar", lambda s, j=j, n=n, par=par, d0=d0: s.activation(
                        out=s_a[par][:, j], in_=s_dt[:, d0 + j],
                        func=AF.Exp, scale=s_amat[:, d0 + j, n:n + 1]),
                        waits=wa + wa2 if j == 0 else (), inc=SS)
                bcb = psC[:].unsqueeze(1).broadcast_to([128, DBG, TC])
                ccb = psD[:].unsqueeze(1).broadcast_to([128, DBG, TC])
                tok_u[n] = pl.op("vector", lambda v, d0=d0, bcb=bcb: v.tensor_mul(
                    out=s_u[:], in0=s_lnx[:, d0:d0 + DBG, :], in1=bcb),
                    waits=W(tok_bbt[n], tok_a,
                            p.get("outlast") if (g == 0 and n == 0) else None),
                    inc=SV)
                pl.op("vector", lambda v, par=par: v.scalar_tensor_tensor(
                    out=s_u[:], in0=s_a[par][:], scalar=1.0, in1=s_u[:],
                    op0=OP.subtract, op1=OP.mult))
                for j in range(DBG):
                    pl.op("vector", lambda v, j=j, n=n, par=par, d0=d0: v.tensor_tensor_scan(
                        out=s_h[:, j], data0=s_a[par][:, j], data1=s_u[:, j],
                        initial=s_carry[:, n, d0 + j:d0 + j + 1],
                        op0=OP.mult, op1=OP.add))
                for j in range(DBG):
                    pl.op("vector", lambda v, n=n, d0=d0, j=j: v.tensor_copy(
                        out=s_carry[:, n, d0 + j:d0 + j + 1],
                        in_=s_h[:, j, TC - 1:TC]))
                for j in range(DBG):
                    tok_w[n] = pl.op("vector", lambda v, j=j, n=n, d0=d0, ccb=ccb: v.scalar_tensor_tensor(
                        out=s_w[:, j], in0=s_h[:, j],
                        scalar=s_inva[:, d0 + j, n:n + 1], in1=psD[:, :],
                        op0=OP.mult, op1=OP.mult),
                        waits=W(tok_cbt[n], tok_sgm.get(n - 1)) if j == 0 else (),
                        inc=SV)
                tok_w_all[q] = tok_w[n]
                if n + 1 < N:
                    for sub in range(2):
                        ssl = slice(sub * SUB, (sub + 1) * SUB)
                        tok_bbt[n + 1] = pl.op("tensor", lambda t, n=n, ssl=ssl: t.matmul(
                            psC[:, ssl],
                            s_identr[0:N, n + 1:n + 2].broadcast_to([N, 128]),
                            s_bcc[0:N, ssl], start=True, stop=True),
                            waits=W(tok_u[n]) if sub == 0 else (), inc=ST)
                tok_sg = None
                for j in range(DBG):
                    py = psA if j == 0 else psB
                    for sub in range(2):
                        ssl = slice(sub * SUB, (sub + 1) * SUB)
                        tok_sg = pl.op("tensor", lambda t, j=j, n=n, ssl=ssl, py=py: t.matmul(
                            py[:, ssl], s_identr[:], s_w[:, j, ssl],
                            start=False, stop=(n == N - 1)),
                            waits=W(tok_w[n], tok_dterm[j]) if (j == 0 and sub == 0) else
                            (W(tok_dterm[j]) if sub == 0 and n == 0 else ()),
                            inc=ST)
                        if n == N - 1:
                            tok_sg_fin[(j, sub)] = tok_sg
                tok_sgm[n] = tok_sg
                if n + 1 < N:
                    for sub in range(2):
                        ssl = slice(sub * SUB, (sub + 1) * SUB)
                        tok_cbt[n + 1] = pl.op("tensor", lambda t, n=n, ssl=ssl: t.matmul(
                            psD[:, ssl],
                            s_identr[0:N, n + 1:n + 2].broadcast_to([N, 128]),
                            s_ccc[:, ssl], start=True, stop=True), inc=ST)
                tok_aa_last = tok_a
            for j in range(DBG):
                py = psA if j == 0 else psB
                for sub in range(2):
                    ssl = slice(sub * SUB, (sub + 1) * SUB)
                    tok_ev = pl.op("vector", lambda v, j=j, py=py, d0=d0, ssl=ssl: v.scalar_tensor_tensor(
                        out=s_xbuf[:, d0 + j, ssl], in0=py[:, ssl], scalar=0.0,
                        in1=s_xbuf[:, d0 + j, ssl], op0=OP.add, op1=OP.add),
                        waits=W(tok_sg_fin[(j, sub)],
                                ck[c]["x"] if g == 0 and j == 0 and sub == 0 else None),
                        inc=SV)
                    ck[c].setdefault("evp2", {})[(d0 + j, sub)] = tok_ev
                ck[c].setdefault("evp", {})[d0 + j] = tok_ev
            for j in range(DBG):
                for sub in range(2):
                    ssl = slice(sub * SUB, (sub + 1) * SUB)
                    ck[c].setdefault("xcp", {})[d0 + j] = pl.dma(
                        "gpsimd", s_lnx[:, d0 + j, ssl], s_xbuf[:, d0 + j, ssl],
                        SXC, waits=W(ck[c]["evp2"][(d0 + j, sub)]))
            prev_u15 = tok_u[N - 1]
            prev_w15 = tok_w[N - 1]
        ck[c]["ev"] = tok_ev
        ck[c]["last_a"] = tok_aa_last
        ck[c]["u15"] = prev_u15
        ck[c]["w15"] = prev_w15

    def P5head(c):
        tok_ev = ck[c]["ev"]
        tok_xcp_last = ck[c]["xcp"][NDB - 1]
        tok_sq2 = {}
        tok_x2r_read = None
        tok_sqS = None
        for sub in range(2):
            ssl = slice(sub * SUB, (sub + 1) * SUB)
            for db in range(NDB):
                par = db % 2
                prev = tok_sq2.get(sub * NDB + db - 2)
                tok_sqS = pl.op("scalar", lambda s, db=db, ssl=ssl, par=par: s.activation(
                    out=s_tsq[:, 2 + par], in_=s_xbuf[:, db, ssl], func=AF.Square),
                    waits=W(tok_ev if (sub == 0 and db == 0) else None, prev),
                    inc=SS)
                tok_x2r_read = pl.op(
                    "tensor", lambda t, db=db, sub=sub, ssl=ssl: t.matmul(
                        psA[0:1, sub * SUB:(sub + 1) * SUB], s_onesc[:, 0:1],
                        s_lnx[:, db, ssl],
                        start=(db == 0), stop=(db == NDB - 1)),
                    waits=W(tok_xcp_last, tok_ev) if (sub == 0 and db == 0)
                    else (), inc=ST)
                tok_sq2[sub * NDB + db] = pl.op("tensor", lambda t, db=db, sub=sub, par=par: t.matmul(
                    psB[0:1, sub * SUB:(sub + 1) * SUB], s_onesc[:, 0:1],
                    s_tsq[:, 2 + par], start=(db == 0), stop=(db == NDB - 1)),
                    waits=W(tok_sqS), inc=ST)
        ck[c]["x2r_read"] = tok_sq2[2 * NDB - 1]
        ck[c]["sqS"] = tok_sqS
        tok_r = pl.op("vector", lambda v: v.tensor_scalar(
            out=row_m, in0=psA[0:1, :], scalar1=1.0 / DM, scalar2=None,
            op0=OP.mult), waits=W(tok_x2r_read, tok_sq2[2 * NDB - 1]), inc=SV)
        tok_r = pl.op("vector", lambda v: v.tensor_scalar(
            out=row_e2, in0=psB[0:1, :], scalar1=1.0 / DM, scalar2=None,
            op0=OP.mult), inc=SV)
        tok_msq = pl.op("scalar", lambda s: s.activation(
            out=row_msq, in_=row_m, func=AF.Square),
            waits=W(tok_r), inc=SS)
        tok_var = pl.op("vector", lambda v: v.tensor_sub(
            out=row_var, in0=row_e2, in1=row_msq),
            waits=W(tok_msq), inc=SV)
        pl.op("scalar", lambda s: s.activation(
            out=row_msq, in_=row_var, func=AF.Ln, bias=s_epsc[0:1, :]),
            waits=W(tok_var), inc=SS)
        tok_rstd = pl.op("scalar", lambda s: s.activation(
            out=s_rstd[:], in_=row_msq, func=AF.Exp, scale=-0.5), inc=SS)
        tok_mrn = pl.op("vector", lambda v: v.scalar_tensor_tensor(
            out=s_mrn[:], in0=row_m, scalar=-1.0, in1=s_rstd[:],
            op0=OP.mult, op1=OP.mult),
            waits=W(tok_rstd, ck[c]["u15"]), inc=SV)
        tok_bc1 = None
        for sub in range(2):
            ssl = slice(sub * SUB, (sub + 1) * SUB)
            tok_bc1 = pl.op("tensor", lambda t, ssl=ssl: t.matmul(
                psC[:, ssl], s_onesr[:], s_rstd[:, ssl], start=True, stop=True),
                waits=W(tok_rstd, tok_mrn) if sub == 0 else (), inc=ST)
            tok_bc1 = pl.op("tensor", lambda t, ssl=ssl: t.matmul(
                psD[:, ssl], s_onesr[:], s_mrn[:, ssl], start=True, stop=True),
                inc=ST)
        ck[c]["bc15"] = tok_bc1
        tok_mb = pl.op("vector", lambda v: v.tensor_copy(out=s_mbrb[:, 0], in_=psC[:, :]),
                       waits=W(tok_bc1, ck[c]["w15"]), inc=SV)
        tok_mb = pl.op("vector", lambda v: v.tensor_copy(out=s_mbrb[:, 1], in_=psD[:, :]),
                       inc=SV)
        ck[c]["mb5"] = tok_mb
        tok_ln2 = None
        for hf in range(2):
            tok_ln2 = pl.op("vector", lambda v, hf=hf: v.tensor_mul(
                out=l2b[hf][:], in0=s_xbuf[:, 4 * hf:4 * hf + 4, :], in1=rb4),
                waits=W(tok_mb, ck[c]["last_a"]) if hf == 0 else (), inc=SV)
            tok_ln2 = pl.op("vector", lambda v, hf=hf: v.tensor_add(
                out=l2b[hf][:], in0=l2b[hf][:], in1=mrn4), inc=SV)
        ck[c]["ln2"] = tok_ln2

    def P5tail(c):
        p = ck[c - 1] if c > 0 else {}
        tok_ln2 = ck[c]["ln2"]
        s_weff = [s_wslab[:, 0], s_wslab[:, 1]]
        tok_wob = {}
        tok_wob[0] = pl.dma("sync", s_weff[0][:], d_weff[:, 0, :, :], SDS[0],
                            waits=W(ck[c]["cc"], ck[c]["bc15"], tok_ln2,
                                    ck[0]["wf_dma"] if c == 0 else None))
        tok_ffn_ev = None
        tok_ev_par = {}
        tok_ffn_mm = {}
        for ob in range(NDB):
            par = ob % 2
            if ob + 1 < NDB:
                tok_wob[ob + 1] = pl.dma(
                    "sync", s_weff[(ob + 1) % 2][:], d_weff[:, ob + 1, :, :],
                    SDS[(ob + 1) % 2],
                    waits=W(tok_ffn_mm.get(ob - 1), tok_ln2 if ob == 0 else None))
            for sub in range(2):
                ssl = slice(sub * SUB, (sub + 1) * SUB)
                gidx = 2 * ob + sub
                pf = psA if gidx % 2 == 0 else psB
                tok_mm = None
                for kb in range(NDB):
                    tok_mm = pl.op("tensor", lambda t, ob=ob, kb=kb, ssl=ssl, pf=pf, par=par: t.matmul(
                        pf[:, 0:SUB], s_weff[par][:, kb, :],
                        l2b[kb // 4][:, kb % 4, ssl],
                        start=(kb == 0), stop=(kb == NDB - 1)),
                        waits=W(tok_ln2, tok_wob[ob],
                                tok_ev_par.get(gidx - 2) if gidx >= 2
                                else ck[c]["x2r_read"])
                        if kb == 0 else (),
                        inc=ST)
                tok_ffn_mm[ob] = tok_mm
                slot = (s_h if ob % 4 < 2 else s_u)[:, ob % 2, :]
                tok_ffn_ev = pl.op("vector", lambda v, ob=ob, ssl=ssl, pf=pf, slot=slot: v.scalar_tensor_tensor(
                    out=slot[:, ssl], in0=pf[:, 0:SUB],
                    scalar=s_bffn[:, ob:ob + 1], in1=s_xbuf[:, ob, ssl],
                    op0=OP.add, op1=OP.add),
                    waits=W(tok_mm, ck[c].get("out%d" % (ob - 4))), inc=SV)
                tok_ev_par[gidx] = tok_ffn_ev
            ck[c]["out%d" % ob] = pl.dma(
                "sync", out_v[:, ob, c * TC:(c + 1) * TC], slot, SO,
                waits=W(tok_ffn_ev))
        ck[c]["outlast"] = ck[c]["out%d" % (NDB - 1)]
        ck[c]["x2_done"] = tok_ffn_ev
        ck[c]["ffnmm"] = tok_ffn_mm[NDB - 1]

    P0_xr(0)
    P1(0)
    P2(0)
    LNX(0)
    P3(0)
    P0_x(0)
    for c in range(NCH):
        P4(c)
        P5head(c)
        if c + 1 < NCH:
            P0_xr(c + 1)
            P1(c + 1)
            LNX(c + 1)
            P2(c + 1)
            P3(c + 1)
        P5tail(c)
        if c + 1 < NCH:
            P0_x(c + 1)

    n_out = pl.cnt[SO]

    sem_names = [SDW, SGW, SWT, SDF, SDX, SGX, SDC, SXC, SDS[0], SDS[1],
                 ST, SS, SV, SO]
    with es, nc.Block() as block:
        sems = {nm: es.enter_context(nc.semaphore("sem_" + nm)) for nm in sem_names}

        def replay(eng, name):
            for waits, emit, inc, inc_amt in pl.streams[name]:
                for (s, v) in waits:
                    eng.wait_ge(sems[s], v)
                inst = emit(eng)
                if inc is not None:
                    inst.then_inc(sems[inc], inc_amt)

        @block.sync
        def _(sync):
            replay(sync, "sync")
            sync.wait_ge(sems[SO], n_out)

        @block.gpsimd
        def _(gpsimd):
            replay(gpsimd, "gpsimd")

        @block.scalar
        def _(scalar):
            replay(scalar, "scalar")

        @block.vector
        def _(vector):
            replay(vector, "vector")

        @block.tensor
        def _(tensor):
            replay(tensor, "tensor")

    return nc


_NC_CACHE = {}


def kernel(**inputs):
    x = np.asarray(inputs["L0_attn_x"], dtype=np.float32)
    A2 = np.ascontiguousarray(np.asarray(inputs["L0_attn_A"], np.float32).reshape(DM, N))
    common = {
        "w_in": np.ascontiguousarray(np.asarray(inputs["L0_attn_W_in"], np.float32)),
        "b_in_c": np.ascontiguousarray(np.asarray(inputs["L0_attn_b_in"], np.float32).reshape(2 * N, 1)),
        "w_dt1": np.ascontiguousarray(np.asarray(inputs["L0_attn_W_dt1"], np.float32)),
        "b_dt1_c": np.ascontiguousarray(np.asarray(inputs["L0_attn_b_dt1"], np.float32).reshape(RDT, 1)),
        "w_dt2": np.ascontiguousarray(np.asarray(inputs["L0_attn_W_dt2"], np.float32)),
        "b_dt2_c": np.ascontiguousarray(np.asarray(inputs["L0_attn_b_dt2"], np.float32).reshape(NDB, 128).T),
        "a_mat": A2,
        "d_vec": np.ascontiguousarray(np.asarray(inputs["L0_attn_D"], np.float32).reshape(NDB, 128).T),
        "wt_ffn": np.ascontiguousarray(np.asarray(inputs["L0_ffn_W"], np.float32).T),
        "b_ffn_c": np.ascontiguousarray(np.asarray(inputs["L0_ffn_b"], np.float32).reshape(NDB, 128).T),
        "lora_down_m": np.ascontiguousarray(np.asarray(inputs["L0_ffn_lora_down"], np.float32)),
        "lora_up_m": np.ascontiguousarray(np.asarray(inputs["L0_ffn_lora_up"], np.float32)),
        "up_aux_t": np.ascontiguousarray(np.asarray(inputs["L0_ffn_up_aux"], np.float32).T),
        "down_aux_m": np.ascontiguousarray(np.asarray(inputs["L0_ffn_down_aux"], np.float32)),
        "ident01": np.eye(128, dtype=np.float32),
        "ones_row": np.ones((1, 128), np.float32),
        "ones_col": np.ones((128, 8), np.float32),
        "eps_col": np.full((128, 1), EPS, np.float32),
    }
    in_maps = []
    for c in range(NCORE):
        m = dict(common)
        m["x_t"] = np.ascontiguousarray(x[c].T)
        in_maps.append(m)

    if "nc" not in _NC_CACHE:
        _NC_CACHE["nc"] = build_program()
    nc = _NC_CACHE["nc"]
    res = run_bass_kernel_spmd(nc, in_maps, core_ids=list(range(NCORE)))
    out = np.stack([np.asarray(res.results[c]["out_t"]).T for c in range(NCORE)], axis=0)
    return out
